# version 01
# baseline (speedup 1.0000x reference)
import numpy as np
import ml_dtypes
from contextlib import ExitStack

import concourse.bass as bass
import concourse.mybir as mybir
from concourse.bass import IndirectOffsetOnAxis
from concourse.bass_utils import run_bass_kernel_spmd

F32 = mybir.dt.float32
BF16 = mybir.dt.bfloat16
I32 = mybir.dt.int32
AF = mybir.ActivationFunctionType
ALU = mybir.AluOpType
AX = mybir.AxisListType
BFD = ml_dtypes.bfloat16

B, S, D = 16, 2048, 1024
NCORES = 8
BPC = B // NCORES
TOK = BPC * S
NT = TOK // 128
NTB = S // 128
H, DK = 16, 64
HI, DI = 4, 64
KSEL = 512
E, FF = 8, 4096
TSLOT = 512
NTILE_E = TOK // TSLOT + E - 1
SLOTR = 8192
NEG = -3.0e38
BIS_B = 256.0
BIS_J = 27
LN_EPS = 1e-5


class Prog:

    ENGINES = ("sync", "gpsimd", "vector", "scalar", "tensor")

    def __init__(self, nc, ctx):
        self.nc = nc
        self.steps = {e: [] for e in self.ENGINES}
        self.sems = {}
        self.counts = {}
        self.ctx = ctx
        self._bar = 0

    def sem(self, name):
        if name not in self.sems:
            self.sems[name] = self.ctx.enter_context(self.nc.semaphore(name))
            self.counts[name] = 0
        return self.sems[name]

    def add(self, eng, fn, waits=(), inc=None, n=1):
        tok = None
        if inc is not None:
            self.sem(inc)
            self.counts[inc] += n
            tok = (inc, self.counts[inc])
        waits = tuple(w for w in waits if w is not None)
        for w in waits:
            self.sem(w[0])
        self.steps[eng].append((fn, waits, inc, n))
        return tok

    def dma(self, eng, out, in_, waits=(), inc="ld", **kw):
        return self.add(eng, lambda e: e.dma_start(out, in_, **kw),
                        waits=waits, inc=inc, n=16)

    def barrier(self):
        self._bar += 1
        toks = {}
        for e in self.ENGINES:
            def f(eng, e=e):
                if e in ("vector", "scalar", "tensor", "gpsimd"):
                    eng.drain()
                return eng.nop()
            toks[e] = self.add(e, f, inc="bar_" + e)
        for e in self.ENGINES:
            self.add(e, lambda eng: None,
                     waits=[toks[o] for o in self.ENGINES if o != e])

    def emit(self, block):
        def mk(eng_name):
            def body(engine):
                waited = {}
                for fn, waits, inc, n in self.steps[eng_name]:
                    for semname, cnt in waits:
                        if waited.get(semname, -1) >= cnt:
                            continue
                        engine.wait_ge(self.sems[semname], cnt)
                        waited[semname] = cnt
                    ins = fn(engine)
                    if inc is not None:
                        if ins is None:
                            ins = engine.nop()
                        ins.then_inc(self.sems[inc], n)
            return body
        block.sync(mk("sync"))
        block.gpsimd(mk("gpsimd"))
        block.vector(mk("vector"))
        block.scalar(mk("scalar"))
        block.tensor(mk("tensor"))


def host_consts():
    c = {}
    d = np.arange(64)
    f = d // 2
    inv = 1.0 / 10000.0 ** (2.0 * f / 64.0)
    t = np.arange(S, dtype=np.float64)
    ang = t[None, :] * inv[:, None]
    cos = np.cos(ang)
    sin = np.sin(ang)
    sgn = np.where(d % 2 == 0, -1.0, 1.0)[:, None]
    c["ropeC"] = np.concatenate([cos, cos], 0).astype(BFD)
    c["ropeS"] = np.concatenate([sin * sgn, sin * sgn], 0).astype(BFD)
    sw = np.zeros((128, 128), np.float32)
    sw[np.arange(128), np.arange(128) ^ 1] = 1.0
    c["pswap"] = sw.astype(BFD)
    c["identb"] = np.eye(128, dtype=BFD)
    c["identf"] = np.eye(128, dtype=np.float32)
    c["lstrict"] = np.triu(np.ones((128, 128), np.float32), 1).astype(BFD)
    tri = np.tril(np.ones((128, 128), np.float32))
    c["tri01"] = tri.astype(np.float32)
    c["trineg"] = ((1.0 - tri) * NEG).astype(np.float32)
    c["iota8"] = np.broadcast_to(np.arange(8, dtype=np.float32), (128, 8)).copy()
    c["iota15"] = np.arange(NTILE_E, dtype=np.float32).reshape(1, NTILE_E)
    c["ones_col_bf"] = np.ones((128, 1), BFD)
    c["ones_row"] = np.ones((1, 128), np.float32)
    pp = np.arange(128, dtype=np.float32)
    c["iota_dc"] = (np.arange(8, dtype=np.float32)[None, :] * 128 + pp[:, None])
    c["iota_h2"] = (np.arange(2, dtype=np.float32)[None, :] * 128 + pp[:, None])
    c["iota_p"] = pp.reshape(128, 1).copy()
    c["cvals"] = np.broadcast_to(
        np.array([LN_EPS, -0.5, -1.0, 0.5, 0.125, 0.0, 1.0, 2.0],
                 dtype=np.float32), (128, 8)).copy()
    return c


def build(stage="full"):
    nc = bass.Bass("TRN2", target_bir_lowering=False)
    ap = lambda t: t.ap()
    x_in = ap(nc.dram_tensor("x_in", [TOK, D], F32, kind="ExternalInput"))
    wqkvT = ap(nc.dram_tensor("wqkvT", [D, 3 * D], F32, kind="ExternalInput"))
    woT = ap(nc.dram_tensor("woT", [D, D], F32, kind="ExternalInput"))
    wqiT = ap(nc.dram_tensor("wqiT", [D, HI * DI], F32, kind="ExternalInput"))
    wkiT = ap(nc.dram_tensor("wkiT", [D, DI], F32, kind="ExternalInput"))
    wwiT = ap(nc.dram_tensor("wwiT", [D, HI], F32, kind="ExternalInput"))
    wr_in = ap(nc.dram_tensor("wr_in", [D, E], F32, kind="ExternalInput"))
    br_in = ap(nc.dram_tensor("br_in", [1, E], F32, kind="ExternalInput"))
    lng_in = ap(nc.dram_tensor("lng_in", [1, D], F32, kind="ExternalInput"))
    lnb_in = ap(nc.dram_tensor("lnb_in", [1, D], F32, kind="ExternalInput"))
    w1_in = ap(nc.dram_tensor("w1_in", [E, D, FF], F32, kind="ExternalInput"))
    b1_in = ap(nc.dram_tensor("b1_in", [E, FF], F32, kind="ExternalInput"))
    w2_in = ap(nc.dram_tensor("w2_in", [E, FF, D], F32, kind="ExternalInput"))
    b2_in = ap(nc.dram_tensor("b2_in", [E, D], F32, kind="ExternalInput"))
    x_out = ap(nc.dram_tensor("x_out", [TOK, D], F32, kind="ExternalOutput"))

    dram = lambda name, shape, dt: ap(nc.dram_tensor(name, shape, dt,
                                                     kind="Internal"))
    xbuf = dram("xbuf", [TOK, D], F32)
    xlnbuf = dram("xlnbuf", [TOK, D], BF16)
    slotbuf = dram("slotbuf", [SLOTR, D], BF16)
    yslots = dram("yslots", [SLOTR, D], BF16)
    w1s = dram("w1s", [E * 8 * 128, FF], BF16)
    w2s = dram("w2s", [E * 2 * 128, 16 * D], BF16)
    wqkvT_bf = dram("wqkvT_bf", [D, 3 * D], BF16)
    woT_bf = dram("woT_bf", [D, D], BF16)
    wqiT_bf = dram("wqiT_bf", [D, HI * DI], BF16)
    wkiT_bf = dram("wkiT_bf", [D, DI], BF16)
    wwiT_bf = dram("wwiT_bf", [D, HI], BF16)
    khD = dram("khD", [H, DK, S], BF16)
    qhD = dram("qhD", [H, DK, S], BF16)
    vtokD = dram("vtokD", [S, H * 65], BF16)

    consts = host_consts()
    cst = {k: ap(nc.inline_tensor(v, name="c_" + k)) for k, v in consts.items()}

    ctx = ExitStack()
    with ctx:
        Es = ctx.enter_context
        p = Prog(nc, ctx)
        sb = lambda name, shape, dt: Es(nc.sbuf_tensor(name, shape, dt))

        ropeC = sb("ropeC", [128, S], BF16)
        ropeS = sb("ropeS", [128, S], BF16)
        pswap = sb("pswap", [128, 128], BF16)
        identb = sb("identb", [128, 128], BF16)
        identf = sb("identf", [128, 128], F32)
        lstrict = sb("lstrict", [128, 128], BF16)
        tri01 = sb("tri01", [128, 128], F32)
        trineg = sb("trineg", [128, 128], F32)
        iota8 = sb("iota8", [128, 8], F32)
        iota15 = sb("iota15", [1, NTILE_E], F32)
        iota_dc = sb("iota_dc", [128, 8], F32)
        iota_h2 = sb("iota_h2", [128, 2], F32)
        iota_p = sb("iota_p", [128, 1], F32)
        cvals = sb("cvals", [128, 8], F32)
        ones_col_bf = sb("ones_col_bf", [128, 1], BF16)
        ones_row = sb("ones_row", [1, 128], F32)
        wr_bf = sb("wr_bf", [128, 8 * E], BF16)
        br_bc = sb("br_bc", [128, E], F32)
        g_bc = sb("g_bc", [128, D], F32)
        b_bc = sb("b_bc", [128, D], F32)
        zeros8 = sb("zeros8", [128, 8], F32)
        zrow = sb("zrow", [1, NTILE_E], F32)
        reg_idx = sb("reg_idx", [128, NT], F32)
        reg_rank = sb("reg_rank", [128, NT], F32)
        reg_gate = sb("reg_gate", [128, NT], F32)
        reg_slot = sb("reg_slot", [128, NT], I32)
        base_row = sb("base_row", [1, E], F32)
        seg_row = sb("seg_row", [1, E], F32)
        te_f = sb("te_f", [1, NTILE_E], F32)
        te_i = sb("te_i", [1, NTILE_E], I32)
        lnsc = sb("lnsc", [128, 8], F32)
        rsc = sb("rsc", [128, 16], F32)
        rsc2 = sb("rsc2", [128, 16], F32)
        rsci = sb("rsci", [128, 8], I32)
        srow = sb("srow", [1, 80], F32)
        srow_i = sb("srow_i", [1, 16], I32)
        jnkA = sb("jnkA", [128, D], F32)
        jnkV = sb("jnkV", [128, D], F32)
        ygb = sb("ygb", [128, D], BF16)
        slotf = sb("slotf", [128, NT], F32)
        xlb2 = sb("xlb2", [128, D], BF16)
        xs2 = sb("xs2", [128, D], F32)

        psA = Es(nc.psum_tensor("psA", [128, 2048], F32))
        ps = [psA[:, i * 512:(i + 1) * 512] for i in range(4)]
        ps.append(Es(nc.psum_tensor("ps4", [128, 512], F32)))
        ps.append(Es(nc.psum_tensor("ps5", [128, 512], F32)))
        pst = Es(nc.psum_tensor("pst", [128, 256], BF16))
        psm = Es(nc.psum_tensor("psm", [128, 512], F32))
        pstA = pst[:, 0:128]
        pstB = pst[:, 128:256]

        for name, tile in (("ropeC", ropeC), ("ropeS", ropeS), ("pswap", pswap),
                           ("identb", identb), ("identf", identf),
                           ("lstrict", lstrict), ("tri01", tri01),
                           ("trineg", trineg), ("iota8", iota8),
                           ("iota15", iota15), ("ones_col_bf", ones_col_bf),
                           ("ones_row", ones_row), ("iota_dc", iota_dc),
                           ("iota_h2", iota_h2), ("iota_p", iota_p),
                           ("cvals", cvals)):
            t_cv = p.dma("sync", tile[:], cst[name])
        wr_f = sb("wr_f", [128, 8 * E], F32)
        t_wr = p.dma("sync", wr_f[:].rearrange("p (c e) -> p c e", c=8),
                     wr_in.rearrange("(c p) e -> p c e", p=128))
        t_wrb = p.add("vector", lambda v: v.tensor_copy(wr_bf[:], wr_f[:]),
                      waits=[t_wr], inc="mv")
        g_row = sb("g_row", [1, D], F32)
        b_row = sb("b_row", [1, D], F32)
        br_row = sb("br_row", [1, E], F32)
        t_g = p.dma("sync", g_row[:], lng_in)
        t_b = p.dma("sync", b_row[:], lnb_in)
        t_br = p.dma("sync", br_row[:], br_in)
        t_z8 = p.add("vector", lambda v: v.memset(zeros8[:], 0.0), inc="mv")
        t_zr = p.add("vector", lambda v: v.memset(zrow[:], 0.0), inc="mv")

        _bcg = [None]
        def bcast_row(dst, src_row, width, wait_tok):
            last = _bcg[0]
            for j in range(0, width, 512):
                w = min(512, width - j)
                tmm = p.add("tensor", lambda e, j=j, w=w: e.matmul(
                    psm[:, 0:w], ones_row[:], src_row[:, j:j + w],
                    start=True, stop=True),
                    waits=[wait_tok] + ([last] if last else []), inc="mp")
                last = p.add("vector", lambda v, j=j, w=w: v.tensor_copy(
                    dst[:, j:j + w], psm[:, 0:w]), waits=[tmm], inc="mv")
            _bcg[0] = last
            return last

        import os
        _ph = os.environ.get("KPHASE", "")
        if _ph == "consts":
            t_gbc = t_bbc = t_brbc = None
        elif _ph == "bcast1":
            t_gbc = bcast_row(g_bc, g_row, D, t_g)
            t_bbc = t_brbc = None
        elif _ph == "bcast2":
            t_brbc = bcast_row(br_bc, br_row, E, t_br)
            t_gbc = t_bbc = None
        else:
            t_gbc = bcast_row(g_bc, g_row, D, t_g)
            t_bbc = bcast_row(b_bc, b_row, D, t_b)
            t_brbc = bcast_row(br_bc, br_row, E, t_br)

        CVW = 1024
        cw = [sb(f"cw{i}", [128, CVW], F32) for i in range(2)]
        cb = [sb(f"cb{i}", [128, CVW], BF16) for i in range(2)]
        conv = []
        for e in range(E):
            for dc in range(8):
                r0 = (e * 8 + dc) * 128
                conv.append((w1_in[e, dc * 128:(dc + 1) * 128, :], FF,
                             [(w1s[r0:r0 + 128, :], 0, FF)]))
        for e in range(E):
            for fc in range(32):
                r0 = (e * 2 + fc // 16) * 128
                c0 = (fc % 16) * D
                conv.append((w2_in[e, fc * 128:(fc + 1) * 128, :], D,
                             [(w2s[r0:r0 + 128, c0:c0 + D], 0, D)]))
        for src, dst, w in ((wqkvT, wqkvT_bf, 3 * D), (woT, woT_bf, D),
                            (wqiT, wqiT_bf, HI * DI), (wkiT, wkiT_bf, DI),
                            (wwiT, wwiT_bf, HI)):
            for dc in range(8):
                conv.append((src[dc * 128:(dc + 1) * 128, :], w,
                             [(dst[dc * 128:(dc + 1) * 128, :], 0, w)]))
        _nj = int(os.environ.get("KJOBS", "0"))
        if _ph in ("consts", "bcast", "bcast1", "bcast2"):
            conv = []
        if stage == "s1x":
            conv = conv[:2]
        jobs = []
        for srcap, w, dsts in conv:
            for lo in range(0, w, CVW):
                hi = min(lo + CVW, w)
                for dstap, dlo, dhi in dsts:
                    a, b_ = max(lo, dlo), min(hi, dhi)
                    if a < b_:
                        jobs.append((srcap[:, a:b_],
                                     dstap[:, a - dlo:b_ - dlo], b_ - a))
        if _nj:
            jobs = jobs[:_nj]
        prev_done = [None, None]
        if not jobs:
            prev_done = [("ld", p.counts.get("ld", 0))] * 2
        for ji, (srcap, dstap, w) in enumerate(jobs):
            bi = ji % 2
            tl = p.dma("sync", cw[bi][:, 0:w], srcap,
                       waits=[prev_done[bi]] if prev_done[bi] else [])
            w2_ = max(16, (w // 2) // 16 * 16)
            segs = [(0, min(w2_, w), "vector"), (min(w2_, w), w, "gpsimd")]
            ctoks = []
            for lo, hi, engn in segs:
                if hi <= lo:
                    continue
                ctoks.append(p.add(
                    engn,
                    lambda v, bi=bi, lo=lo, hi=hi: v.tensor_copy(
                        cb[bi][:, lo:hi], cw[bi][:, lo:hi]),
                    waits=[tl], inc="mv" if engn == "vector" else "mg"))
            prev_done[bi] = p.dma("sync", dstap, cb[bi][:, 0:w],
                                  waits=ctoks, inc="st")
        t_conv = prev_done[(len(jobs) - 1) % 2]
        t_conv2 = prev_done[(len(jobs)) % 2]

        def ln_pass(xt, out, wait_toks, affine=False):
            sumx, sumsq = lnsc[:, 0:1], lnsc[:, 1:2]
            negmu, varv = lnsc[:, 2:3], lnsc[:, 3:4]
            lnv, rstd = lnsc[:, 4:5], lnsc[:, 5:6]
            t1 = p.add("scalar", lambda s: s.activation(
                jnkA[:], xt[:], AF.Identity, accum_out=sumx),
                waits=list(wait_toks), inc="ms")
            t2 = p.add("vector", lambda v: v.scalar_tensor_tensor(
                jnkV[:], xt[:], 1.0, xt[:], op0=ALU.bypass, op1=ALU.mult,
                accum_out=sumsq), waits=list(wait_toks), inc="mv")
            def smallsA(v):
                v.tensor_scalar_mul(negmu, sumx, -1.0 / D)
                return v.tensor_scalar_mul(varv, sumsq, 1.0 / D)
            t3a = p.add("vector", smallsA, waits=[t1, t2], inc="mv")
            t3b = p.add("vector", lambda v: v.tensor_mul(
                lnsc[:, 6:7], negmu, negmu), waits=[t3a], inc="mv")
            t3 = p.add("vector", lambda v: v.tensor_sub(
                varv, varv, lnsc[:, 6:7]), waits=[t3b], inc="mv")
            t4 = p.add("scalar", lambda s: s.activation(
                lnv, varv, AF.Ln, bias=cvals[:, 0:1]), waits=[t3, t_cv],
                inc="ms")
            t5 = p.add("scalar", lambda s: s.activation(
                rstd, lnv, AF.Exp, scale=cvals[:, 1:2]), waits=[t4],
                inc="ms")
            t6 = p.add("vector", lambda v: v.tensor_scalar(
                out[:], xt[:], negmu, rstd, op0=ALU.add, op1=ALU.mult),
                waits=[t5], inc="mv")
            if affine:
                t7 = p.add("vector", lambda v: v.tensor_mul(out[:], out[:],
                                                            g_bc[:]),
                           waits=[t6], inc="mv")
                t6 = p.add("vector", lambda v: v.tensor_add(out[:], out[:],
                                                            b_bc[:]),
                           waits=[t7], inc="mv")
            return t6


        def ffn_call(call, x_src, x_dst, stk):
            cn = f"f{call}"
            sbf = lambda name, shape, dt: stk.enter_context(
                nc.sbuf_tensor(name, shape, dt))
            xs = sbf(f"{cn}_xs", [128, D], F32)
            xf = sbf(f"{cn}_xf", [128, D], F32)
            xl = sbf(f"{cn}_xl", [128, D], F32)
            xlb = sbf(f"{cn}_xlb", [128, D], BF16)
            xfb = sbf(f"{cn}_xfb", [128, D], BF16)
            xfT = sbf(f"{cn}_xfT", [128, 8, 128], BF16)
            lg = sbf(f"{cn}_lg", [128, 8], F32)
            oh = sbf(f"{cn}_oh", [128, 8], F32)
            ohb = sbf(f"{cn}_ohb", [128, 8], BF16)

            t_base = p.add("vector", lambda v: v.memset(base_row[:], 0.0),
                           inc="mv")
            prev_counts = t_base
            st_xl = [None] * NT
            prev_tile = [None]
            guard = {"pstA": None, "psm": _bcg[0], "ps0": None, "ps1": None,
                     "ps2": None}
            for i in range(NT):
                tl = p.dma("sync", xs[:], x_src[i * 128:(i + 1) * 128, :],
                           waits=[prev_tile[0]] if prev_tile[0] else [])
                t_ln1 = ln_pass(xs, xf, [tl])
                t_ln2 = ln_pass(xf, xl, [t_ln1], affine=True)
                t_xlb = p.add("vector", lambda v: v.tensor_copy(xlb[:], xl[:]),
                              waits=[t_ln2], inc="mv")
                st_xl[i] = p.dma("sync", xlnbuf[i * 128:(i + 1) * 128, :],
                                 xlb[:], waits=[t_xlb], inc="st")
                import os as _os
                _sub = _os.environ.get("KSUB", "e")
                if _sub == "a":
                    prev_tile[0] = st_xl[i]
                    continue
                t_xfb = p.add("vector", lambda v: v.tensor_copy(xfb[:], xf[:]),
                              waits=[t_ln1], inc="mv")
                trtoks = []
                for dc in range(8):
                    tt = p.add("tensor", lambda e, dc=dc: e.transpose(
                        pstA, xfb[:, dc * 128:(dc + 1) * 128], identb[:]),
                        waits=[t_xfb, guard["pstA"]], inc="mp")
                    guard["pstA"] = p.add(
                        "vector", lambda v, dc=dc: v.tensor_copy(
                            xfT[:, dc, :], pstA), waits=[tt], inc="mv")
                    trtoks.append(guard["pstA"])
                if _sub == "b1":
                    prev_tile[0] = trtoks[-1]
                    continue
                def rmm(e):
                    last = None
                    for dc in range(8):
                        last = e.matmul(psm[:, 0:8], xfT[:, dc, :],
                                        wr_bf[:, dc * 8:(dc + 1) * 8],
                                        start=(dc == 0), stop=(dc == 7))
                    return last
                t_rmm = p.add("tensor", rmm,
                              waits=trtoks + [t_wrb, t_brbc, guard["psm"]],
                              inc="mp")
                t_lg = p.add("vector", lambda v: v.tensor_add(
                    lg[:], psm[:, 0:8], br_bc[:]), waits=[t_rmm], inc="mv")
                guard["psm"] = t_lg
                if _sub == "b":
                    prev_tile[0] = t_lg
                    continue
                t_mx = p.add("vector", lambda v: v.reduce_max(
                    rsc[:, 0:1], lg[:], axis=AX.X), waits=[t_lg], inc="mv")
                t_nm = p.add("scalar", lambda s: s.activation(
                    rsc[:, 1:2], rsc[:, 0:1], AF.Identity,
                    scale=cvals[:, 2:3]), waits=[t_mx, t_cv], inc="ms")
                t_ex = p.add("scalar", lambda s: s.activation(
                    rsc2[:, 0:8], lg[:], AF.Exp, bias=rsc[:, 1:2],
                    accum_out=rsc[:, 2:3]), waits=[t_nm], inc="ms")
                t_gt = p.add("vector", lambda v: v.reciprocal(
                    rsc[:, 3:4], rsc[:, 2:3]), waits=[t_ex], inc="mv")
                t_gt2 = p.add("scalar", lambda s, i=i: s.activation(
                    reg_gate[:, i:i + 1], rsc[:, 3:4], AF.Identity,
                    scale=cvals[:, 3:4]), waits=[t_gt, t_cv], inc="ms")
                if _sub == "c":
                    prev_tile[0] = t_gt2
                    continue
                t_eq = p.add("vector", lambda v: v.tensor_scalar(
                    oh[:], lg[:], rsc[:, 0:1], None, op0=ALU.is_ge),
                    waits=[t_ex], inc="mv")
                t_pc = p.add("vector", lambda v: v.tensor_tensor_scan(
                    rsc2[:, 8:16], oh[:], zeros8[:], 0.0, op0=ALU.add,
                    op1=ALU.add), waits=[t_eq, t_z8], inc="mv")
                t_fi = p.add("vector", lambda v: v.tensor_scalar(
                    rsc2[:, 8:16], rsc2[:, 8:16], 1.0, None, op0=ALU.is_le),
                    waits=[t_pc], inc="mv")
                t_fa = p.add("vector", lambda v: v.tensor_mul(
                    oh[:], oh[:], rsc2[:, 8:16]), waits=[t_fi], inc="mv")
                t_ix = p.add("vector", lambda v, i=i: v.scalar_tensor_tensor(
                    rsc2[:, 8:16], oh[:], 1.0, iota8[:], op0=ALU.bypass,
                    op1=ALU.mult, accum_out=reg_idx[:, i:i + 1]),
                    waits=[t_fa], inc="mv")
                t_ohb = p.add("vector", lambda v: v.tensor_copy(ohb[:], oh[:]),
                              waits=[t_fa], inc="mv")
                if _sub == "d":
                    prev_tile[0] = t_ohb
                    continue
                t_pf = p.add("tensor", lambda e: e.matmul(
                    ps[0][:, 0:8], lstrict[:], ohb[:], start=True, stop=True),
                    waits=[t_ohb, guard["ps0"]], inc="mp")
                t_ct = p.add("tensor", lambda e: e.matmul(
                    ps[1][0:1, 0:8], ones_col_bf[:], ohb[:], start=True,
                    stop=True), waits=[t_ohb, guard["ps1"]], inc="mp")
                t_bb = p.add("tensor", lambda e: e.matmul(
                    ps[2][:, 0:8], ones_row[:], base_row[:], start=True,
                    stop=True), waits=[prev_counts, t_ohb, guard["ps2"]],
                    inc="mp")
                t_gr = p.add("vector", lambda v: v.tensor_copy(
                    rsc2[:, 0:8], ps[0][:, 0:8]), waits=[t_pf], inc="mv")
                guard["ps0"] = t_gr
                t_gr2 = p.add("vector", lambda v: v.tensor_add(
                    rsc2[:, 0:8], rsc2[:, 0:8], ps[2][:, 0:8]),
                    waits=[t_gr, t_bb], inc="mv")
                guard["ps2"] = t_gr2
                t_rk = p.add("vector", lambda v, i=i: v.scalar_tensor_tensor(
                    jnkV[:, 0:8], rsc2[:, 0:8], 1.0, oh[:], op0=ALU.bypass,
                    op1=ALU.mult, accum_out=reg_rank[:, i:i + 1]),
                    waits=[t_gr2], inc="mv")
                prev_counts = p.add("vector", lambda v: v.tensor_add(
                    base_row[:], base_row[:], ps[1][0:1, 0:8]),
                    waits=[t_ct, t_rk], inc="mv")
                guard["ps1"] = prev_counts
                prev_tile[0] = prev_counts

            if stage == "s1a":
                import os as _os
                dsel = _os.environ.get("KDUMP", "irgb")
                tds = [None]
                if "i" in dsel:
                    tds.append(p.dma("sync", x_dst[0:128, 0:NT], reg_idx[:],
                                     waits=[prev_counts], inc="st"))
                if "r" in dsel:
                    tds.append(p.dma("sync", x_dst[0:128, 64:64 + NT],
                                     reg_rank[:], waits=[prev_counts],
                                     inc="st"))
                if "g" in dsel:
                    tds.append(p.dma("sync", x_dst[0:128, 128:128 + NT],
                                     reg_gate[:], waits=[prev_counts],
                                     inc="st"))
                if "b" in dsel:
                    tds.append(p.dma("sync", x_dst[129:130, 0:E], base_row[:],
                                     waits=[prev_counts], inc="st"))
                return tds[-1]

            t_s1a = p.add("vector", lambda v: v.tensor_scalar(
                srow[:, 0:8], base_row[:], 511.0, 1.0 / TSLOT,
                op0=ALU.add, op1=ALU.mult), waits=[prev_counts], inc="mv")
            t_s1 = p.add("vector", lambda v: v.tensor_scalar_add(
                srow[:, 0:8], srow[:, 0:8], -0.49999), waits=[t_s1a],
                inc="mv")
            t_s2 = p.add("vector", lambda v: v.tensor_copy(
                srow_i[:, 0:8], srow[0:1, 0:8]), waits=[t_s1], inc="mv")
            t_s3 = p.add("vector", lambda v: v.tensor_copy(
                srow[:, 8:16], srow_i[0:1, 0:8]), waits=[t_s2], inc="mv")
            t_s4 = p.add("vector", lambda v: v.tensor_tensor_scan(
                srow[:, 16:24], srow[0:1, 8:16], zrow[0:1, 0:8], 0.0,
                op0=ALU.add, op1=ALU.add), waits=[t_s3, t_zr], inc="mv")
            t_s5 = p.add("vector", lambda v: v.tensor_sub(
                srow[:, 24:32], srow[0:1, 16:24], srow[0:1, 8:16]),
                waits=[t_s4], inc="mv")
            t_s6 = p.add("vector", lambda v: v.tensor_scalar_mul(
                seg_row[:], srow[0:1, 24:32], float(TSLOT)), waits=[t_s5],
                inc="mv")
            t_t0 = p.add("vector", lambda v: v.memset(te_f[:], 0.0),
                         waits=[t_s6], inc="mv")
            t_prev = t_t0
            for e in range(E):
                def temA(v, e=e):
                    v.tensor_scalar(srow[:, 32:32 + NTILE_E], iota15[:],
                                    srow[0:1, 24 + e:25 + e], None,
                                    op0=ALU.is_ge)
                    return v.tensor_scalar(srow[:, 48:48 + NTILE_E],
                                           iota15[:],
                                           srow[0:1, 16 + e:17 + e], None,
                                           op0=ALU.is_lt)
                tA = p.add("vector", temA, waits=[t_prev], inc="mv")
                tB = p.add("vector", lambda v: v.tensor_mul(
                    srow[:, 32:32 + NTILE_E], srow[0:1, 32:32 + NTILE_E],
                    srow[0:1, 48:48 + NTILE_E]), waits=[tA], inc="mv")
                t_prev = p.add("vector", lambda v, e=e: v.scalar_tensor_tensor(
                    te_f[:], srow[0:1, 32:32 + NTILE_E], float(e),
                    te_f[:], op0=ALU.mult, op1=ALU.add), waits=[tB], inc="mv")
            t_te = p.add("vector", lambda v: v.tensor_copy(te_i[:], te_f[:]),
                         waits=[t_prev], inc="mv")

            if stage == "s1b":
                tds = []
                tds.append(p.dma("sync", x_dst[128:129, 0:NTILE_E], te_f[:],
                                 waits=[t_te], inc="st"))
                tds.append(p.dma("sync", x_dst[130:131, 0:E], seg_row[:],
                                 waits=[t_s6], inc="st"))
                return tds[-1]

            t_scat = None
            prev3 = t_te
            guard["ps3"] = None
            for i in range(NT):
                t_sb2 = p.add("tensor", lambda e: e.matmul(
                    ps[3][:, 0:8], ones_row[:], seg_row[:], start=True,
                    stop=True), waits=[prev3, guard["ps3"]], inc="mp")
                t_oh2 = p.add("vector", lambda v, i=i: v.tensor_scalar(
                    rsc2[:, 0:8], iota8[:], reg_idx[:, i:i + 1], None,
                    op0=ALU.is_equal), waits=[prev3], inc="mv")
                t_sl = p.add("vector", lambda v: v.scalar_tensor_tensor(
                    jnkV[:, 0:8], ps[3][:, 0:8], 1.0, rsc2[:, 0:8],
                    op0=ALU.bypass, op1=ALU.mult, accum_out=rsc[:, 0:1]),
                    waits=[t_oh2, t_sb2], inc="mv")
                guard["ps3"] = t_sl
                t_sl2 = p.add("vector", lambda v, i=i: v.tensor_add(
                    rsc[:, 1:2], rsc[:, 0:1], reg_rank[:, i:i + 1]),
                    waits=[t_sl], inc="mv")
                t_sl3 = p.add("vector", lambda v: v.tensor_scalar(
                    rsc[:, 1:2], rsc[:, 1:2], float(SLOTR - 1), 0.0,
                    op0=ALU.min, op1=ALU.max), waits=[t_sl2], inc="mv")
                t_sl4 = p.add("vector", lambda v, i=i: v.tensor_copy(
                    reg_slot[:, i:i + 1], rsc[:, 1:2]), waits=[t_sl3],
                    inc="mv")
                t_xl2 = p.dma("sync", xlb2[:],
                              xlnbuf[i * 128:(i + 1) * 128, :],
                              waits=[st_xl[i], t_scat])
                t_scat = p.add("gpsimd", lambda g, i=i: g.indirect_dma_start(
                    slotbuf, IndirectOffsetOnAxis(reg_slot[:, i:i + 1], 0),
                    xlb2[:], None),
                    waits=[t_sl4, t_xl2], inc="gp", n=16)
                prev3 = t_sl4

            if stage in ("s1", "s1x"):
                tds = []
                tds.append(p.dma("sync", x_dst[0:128, 0:NT], reg_idx[:],
                                 waits=[prev3, t_scat], inc="st"))
                tds.append(p.dma("sync", x_dst[0:128, 64:64 + NT],
                                 reg_rank[:], inc="st"))
                tds.append(p.dma("sync", x_dst[0:128, 128:128 + NT],
                                 reg_gate[:], inc="st"))
                t_sf = p.add("vector", lambda v: v.tensor_copy(
                    slotf[:], reg_slot[:]), waits=[t_scat], inc="mv")
                tds.append(p.dma("sync", x_dst[0:128, 192:192 + NT],
                                 slotf[:], waits=[t_sf], inc="st"))
                tds.append(p.dma("sync", x_dst[128:129, 0:NTILE_E], te_f[:],
                                 waits=[t_te], inc="st"))
                tds.append(p.dma("sync", x_dst[129:130, 0:E], base_row[:],
                                 waits=[prev_counts], inc="st"))
                tds.append(p.dma("sync", x_dst[130:131, 0:E], seg_row[:],
                                 waits=[t_s6], inc="st"))
                tds.append(p.dma("sync", x_dst[0:128, 240:248], lg[:],
                                 waits=[prev_counts], inc="st"))
                tds.append(p.dma("sync", x_dst[0:128, 248:264], rsc[:],
                                 waits=[prev_counts], inc="st"))
                tds.append(p.dma("sync", x_dst[0:128, 264:272],
                                 xf[:, 0:8], waits=[prev_counts], inc="st"))
                tds.append(p.dma("sync", x_dst[0:128, 272:280],
                                 xl[:, 0:8], waits=[prev_counts], inc="st"))
                tds.append(p.dma("sync", x_dst[0:128, 280:288],
                                 lnsc[:], waits=[prev_counts], inc="st"))
                return tds[-1]

            xst = sbf(f"{cn}_xst", [128, 4, D], BF16)
            ytok = xst
            xslotT = sbf(f"{cn}_xslotT", [128, 8, TSLOT], BF16)
            hT = sbf(f"{cn}_hT", [128, 32, TSLOT], BF16)
            yT = sbf(f"{cn}_yT", [128, 8, TSLOT], BF16)
            w1q = [[sbf(f"{cn}_w1q{par}_{dc}", [128, 1024], BF16)
                    for dc in range(8)] for par in range(2)]
            w2q = [sbf(f"{cn}_w2q{par}", [128, 4096], BF16) for par in range(2)]
            b1l = sbf(f"{cn}_b1l", [32, 128], F32)
            b2l = sbf(f"{cn}_b2l", [8, 128], F32)
            b1c = sbf(f"{cn}_b1c", [128, 32], F32)
            b2c = sbf(f"{cn}_b2c", [128, 8], F32)
            w1idx = sbf(f"{cn}_w1idx", [128, 8], I32)
            w2idx = sbf(f"{cn}_w2idx", [128, 2], I32)
            b1idx = sbf(f"{cn}_b1idx", [32, 1], I32)
            b2idx = sbf(f"{cn}_b2idx", [8, 1], I32)
            idxf = sbf(f"{cn}_idxf", [128, 8], F32)
            te_bc = sbf(f"{cn}_tebc", [128, NTILE_E], F32)

            t_tb = p.add("tensor", lambda e: e.matmul(
                ps[4][:, 0:NTILE_E], ones_row[:], te_f[:], start=True,
                stop=True), waits=[t_te], inc="mp")
            t_tbc = p.add("vector", lambda v: v.tensor_copy(
                te_bc[:], ps[4][:, 0:NTILE_E]), waits=[t_tb], inc="mv")

            b1_rows = b1_in.rearrange("e (a p) -> (e a) p", p=128)
            b2_rows = b2_in.rearrange("e (a p) -> (e a) p", p=128)

            prev_y_st = [None] * 4
            prev_xst_done = None
            prev_gath = {}
            prev_w1_use = [[None] * 8 for _ in range(2)]
            prev_w2_use = [None, None]
            prev_wg = [None]
            prev_b_use = None
            hev = [None]
            yev = [None]
            for j in range(NTILE_E):
                row0 = j * TSLOT
                def mkidx(v, j=j):
                    v.tensor_scalar_mul(idxf[:, 0:1], te_bc[:, j:j + 1], 1024.0)
                    v.tensor_scalar_mul(idxf[:, 1:2], te_bc[:, j:j + 1], 256.0)
                    v.tensor_scalar_mul(idxf[:, 2:3], te_bc[:, j:j + 1], 32.0)
                    return v.tensor_scalar_mul(idxf[:, 3:4], te_bc[:, j:j + 1],
                                               8.0)
                t_i0 = p.add("vector", mkidx,
                             waits=[t_tbc, prev_b_use, prev_wg[0]],
                             inc="mv")
                def mkidx2a(v):
                    v.tensor_scalar(rsc2[:, 0:8], iota_dc[:], idxf[:, 0:1],
                                    None, op0=ALU.add)
                    v.tensor_scalar(rsc2[:, 8:10], iota_h2[:], idxf[:, 1:2],
                                    None, op0=ALU.add)
                    v.tensor_scalar(rsc2[0:32, 10:11], iota_p[0:32, :],
                                    idxf[0:32, 2:3], None, op0=ALU.add)
                    return v.tensor_scalar(rsc2[0:8, 11:12], iota_p[0:8, :],
                                           idxf[0:8, 3:4], None, op0=ALU.add)
                t_i1a = p.add("vector", mkidx2a, waits=[t_i0], inc="mv")
                def mkidx2b(v):
                    v.tensor_copy(w1idx[:], rsc2[:, 0:8])
                    v.tensor_copy(w2idx[:], rsc2[:, 8:10])
                    v.tensor_copy(b1idx[:], rsc2[0:32, 10:11])
                    return v.tensor_copy(b2idx[:], rsc2[0:8, 11:12])
                t_i1 = p.add("vector", mkidx2b, waits=[t_i1a], inc="mv")

                tld = []
                for q in range(4):
                    tld.append(p.dma(
                        "sync", xst[:, q, :],
                        slotbuf[row0 + q * 128: row0 + (q + 1) * 128, :],
                        waits=[t_scat, prev_xst_done, prev_y_st[q]]))
                t_bg1 = p.add("gpsimd", lambda g: g.indirect_dma_start(
                    b1l[:], None, b1_rows, IndirectOffsetOnAxis(b1idx[:], 0)),
                    waits=[t_i1, prev_b_use], inc="gp", n=16)
                t_bg2 = p.add("gpsimd", lambda g: g.indirect_dma_start(
                    b2l[:], None, b2_rows, IndirectOffsetOnAxis(b2idx[:], 0)),
                    waits=[t_i1, prev_b_use], inc="gp", n=16)
                t_b1t = p.add("tensor", lambda e: e.transpose(
                    psm[:, 0:32], b1l[:], identf[0:32, 0:32]),
                    waits=[t_bg1, guard["psm"]], inc="mp")
                t_b1e = p.add("vector", lambda v: v.tensor_copy(
                    b1c[:], psm[:, 0:32]), waits=[t_b1t], inc="mv")
                t_b2t = p.add("tensor", lambda e: e.transpose(
                    psm[:, 32:40], b2l[:], identf[0:8, 0:8]),
                    waits=[t_bg2, t_b1e], inc="mp")
                t_b2e = p.add("vector", lambda v: v.tensor_copy(
                    b2c[:], psm[:, 32:40]), waits=[t_b2t], inc="mv")
                guard["psm"] = t_b2e
                prev_b_use = t_b2e

                ttok = []
                prev_ev = None
                for q in range(4):
                    for dc in range(8):
                        tt = p.add("tensor", lambda e, q=q, dc=dc: e.transpose(
                            pstB, xst[:, q, dc * 128:(dc + 1) * 128],
                            identb[:]),
                            waits=[tld[q], prev_ev, guard.get("pstB")],
                            inc="mp")
                        prev_ev = p.add("vector",
                                        lambda v, q=q, dc=dc: v.tensor_copy(
                                            xslotT[:, dc, q * 128:(q + 1) * 128],
                                            pstB), waits=[tt], inc="mv")
                        guard["pstB"] = prev_ev
                        ttok.append(prev_ev)
                prev_xst_done = ttok[-1]


                hev = []
                for q in range(4):
                    par = q % 2
                    w1g = []
                    for dc in range(8):
                        w1g.append(p.add(
                            "gpsimd",
                            lambda g, q=q, dc=dc, par=par:
                            g.indirect_dma_start(
                                w1q[par][dc][:], None, w1s,
                                IndirectOffsetOnAxis(w1idx[:, dc:dc + 1], 0),
                                element_offset=q * 1024),
                            waits=[t_i1, t_conv, t_conv2,
                                   prev_w1_use[par][dc]],
                            inc="gp", n=16))
                    for pr in range(4):
                        fb0 = q * 8 + pr * 2
                        fb1 = fb0 + 1
                        def hmm(e, pr=pr, par=par):
                            last = None
                            for dc in range(8):
                                e.matmul(ps[0][:, 0:TSLOT],
                                         w1q[par][dc][:, pr * 256:
                                                      pr * 256 + 128],
                                         xslotT[:, dc, :],
                                         start=(dc == 0), stop=(dc == 7))
                                last = e.matmul(
                                    ps[1][:, 0:TSLOT],
                                    w1q[par][dc][:, pr * 256 + 128:
                                                 pr * 256 + 256],
                                    xslotT[:, dc, :],
                                    start=(dc == 0), stop=(dc == 7))
                            return last
                        mmw = [w1g[7]] + ttok
                        if hev:
                            mmw.append(hev[-1])
                        t_h = p.add("tensor", hmm, waits=mmw, inc="mp")
                        e0 = p.add("vector", lambda v, fb0=fb0: v.tensor_scalar(
                            hT[:, fb0, :], ps[0][:, 0:TSLOT],
                            b1c[:, fb0:fb0 + 1], 0.0, op0=ALU.add,
                            op1=ALU.max), waits=[t_h, t_b1e], inc="mv")
                        e1 = p.add("scalar", lambda s, fb1=fb1: s.activation(
                            hT[:, fb1, :], ps[1][:, 0:TSLOT], AF.Relu,
                            bias=b1c[:, fb1:fb1 + 1]),
                            waits=[t_h, t_b1e], inc="ms")
                        hev.append(p.add("vector", lambda v: None,
                                         waits=[e0, e1], inc="mv"))
                    for dc in range(8):
                        prev_w1_use[par][dc] = hev[-1]

                yev = []
                for dpg in range(2):
                    grp_mm_last = None
                    qn = 0
                    for hf in range(2):
                        for fq in range(4):
                            par = qn % 2
                            t_g2 = p.add(
                                "gpsimd",
                                lambda g, hf=hf, fq=fq, par=par:
                                g.indirect_dma_start(
                                    w2q[par][:], None, w2s,
                                    IndirectOffsetOnAxis(
                                        w2idx[:, hf:hf + 1], 0),
                                    element_offset=fq * 4096),
                                waits=[t_i1, t_conv, t_conv2, hev[-1],
                                       prev_w2_use[par]],
                                inc="gp", n=16)
                            def ymm(e, hf=hf, fq=fq, dpg=dpg, par=par,
                                    first=(qn == 0)):
                                last = None
                                for fcl in range(4):
                                    fb = hf * 16 + fq * 4 + fcl
                                    st_ = first and fcl == 0
                                    sp_ = (hf == 1 and fq == 3 and fcl == 3)
                                    for b_ in range(4):
                                        last = e.matmul(
                                            ps[2 + b_][:, 0:TSLOT],
                                            w2q[par][:, fcl * D + dpg * 512 +
                                                      b_ * 128:
                                                      fcl * D + dpg * 512 +
                                                      b_ * 128 + 128],
                                            hT[:, fb, :], start=st_,
                                            stop=sp_)
                                return last
                            mmw = [t_g2, hev[-1]]
                            if grp_mm_last is not None:
                                mmw.append(grp_mm_last)
                            elif yev:
                                mmw.append(yev[-1])
                            grp_mm_last = p.add("tensor", ymm, waits=mmw,
                                                inc="mp")
                            prev_w2_use[par] = grp_mm_last
                            prev_wg[0] = t_g2
                            qn += 1
                    evs_y = []
                    for b_ in range(4):
                        dblk = dpg * 4 + b_
                        if b_ % 2 == 0:
                            evs_y.append(p.add(
                                "vector", lambda v, dblk=dblk, b_=b_:
                                v.tensor_scalar(
                                    yT[:, dblk, :], ps[2 + b_][:, 0:TSLOT],
                                    b2c[:, dblk:dblk + 1], None, op0=ALU.add),
                                waits=[grp_mm_last, t_b2e], inc="mv"))
                        else:
                            evs_y.append(p.add(
                                "scalar", lambda s, dblk=dblk, b_=b_:
                                s.activation(
                                    yT[:, dblk, :], ps[2 + b_][:, 0:TSLOT],
                                    AF.Identity, bias=b2c[:, dblk:dblk + 1]),
                                waits=[grp_mm_last, t_b2e], inc="ms"))
                    yev.append(p.add("vector", lambda v: None,
                                     waits=evs_y, inc="mv"))

                for q in range(4):
                    evs = []
                    for dc in range(8):
                        tt = p.add("tensor", lambda e, q=q, dc=dc: e.transpose(
                            pstA, yT[:, dc, q * 128:(q + 1) * 128], identb[:]),
                            waits=[yev[-1], guard["pstA"]], inc="mp")
                        guard["pstA"] = p.add(
                            "vector", lambda v, q=q, dc=dc: v.tensor_copy(
                                ytok[:, q, dc * 128:(dc + 1) * 128],
                                pstA), waits=[tt], inc="mv")
                        evs.append(guard["pstA"])
                    prev_y_st[q] = p.dma(
                        "sync", yslots[row0 + q * 128: row0 + (q + 1) * 128, :],
                        ytok[:, q, :], waits=evs + [prev_y_st[q]], inc="st")
            t_allst = p.add("sync", lambda e: None,
                            waits=[t for t in prev_y_st if t])

            t_fin = None
            st_tok = ("st", p.counts["st"])
            for i in range(NT):
                t_g1 = p.add("gpsimd", lambda g, i=i: g.indirect_dma_start(
                    ygb[:], None, yslots,
                    IndirectOffsetOnAxis(reg_slot[:, i:i + 1], 0)),
                    waits=[st_tok, t_fin], inc="gp", n=16)
                t_x = p.dma("sync", xs2[:], x_src[i * 128:(i + 1) * 128, :],
                            waits=[t_fin] if t_fin else [])
                t_up = p.add("vector", lambda v, i=i: v.scalar_tensor_tensor(
                    xs2[:], ygb[:], reg_gate[:, i:i + 1], xs2[:],
                    op0=ALU.mult, op1=ALU.add), waits=[t_g1, t_x], inc="mv")
                t_fin = p.dma("sync", x_dst[i * 128:(i + 1) * 128, :], xs2[:],
                              waits=[t_up], inc="st")
            return t_fin

        def attention(stk):
            woT_sb = stk.enter_context(
                nc.sbuf_tensor("a_woT", [128, 8 * D], BF16))
            wo_toks = []
            for hc in range(8):
                wo_toks.append(p.dma(
                    "sync", woT_sb[:, hc * D:(hc + 1) * D],
                    woT_bf[hc * 128:(hc + 1) * 128, :],
                    waits=[t_conv, t_conv2]))
            t_wo = wo_toks[-1]
            for b_ in range(BPC):
                attn_batch(b_ * S, t_wo, woT_sb)

        def attn_batch(xoff, t_wo, woT_sb):
            bctx = ExitStack()
            with bctx:
                sbl = lambda name, shape, dt: bctx.enter_context(
                    nc.sbuf_tensor(f"{name}_{xoff}", shape, dt))
                vbuf = sbl("a_vbuf", [128, 16, H * 65], BF16)
                qi_sb = [sbl(f"a_qi{ih}", [64, S], BF16) for ih in range(HI)]
                ki_sb = sbl("a_ki", [64, S], BF16)
                wi_tok = sbl("a_wi", [128, NTB, HI], F32)
                aguard = {"pstA": None, "pstB": None, "psm": None,
                          "psA": None, "ps4": None, "ps5": None}

                pctx = ExitStack()
                with pctx:
                    sbp = lambda name, shape, dt: pctx.enter_context(
                        nc.sbuf_tensor(f"{name}_{xoff}", shape, dt))
                    xaT = sbp("a_xaT", [128, 8, S], BF16)
                    xs3 = sbp("a_xs3", [128, D], F32)
                    xa = sbp("a_xa", [128, D], F32)
                    xab = sbp("a_xab", [128, D], BF16)
                    wpb = [sbp(f"a_wpb{k}", [128, 128], BF16) for k in range(2)]
                    praw = sbp("a_praw", [128, S], BF16)
                    prw2 = sbp("a_prw2", [128, 512], BF16)
                    prw3 = sbp("a_prw3", [128, 512], BF16)
                    stg = sbp("a_stg", [64, S], BF16)
                    wi32 = sbp("a_wi32", [32, S], F32)

                    t_prev = None
                    for tb in range(NTB):
                        tl = p.dma("sync", xs3[:],
                                   xbuf[xoff + tb * 128: xoff + (tb + 1) * 128, :],
                                   waits=[t_prev])
                        t_ln = ln_pass(xs3, xa, [tl])
                        t_ab = p.add("vector", lambda v: v.tensor_copy(
                            xab[:], xa[:]), waits=[t_ln], inc="mv")
                        for dc in range(8):
                            tt = p.add("tensor", lambda e, dc=dc: e.transpose(
                                pstA, xab[:, dc * 128:(dc + 1) * 128],
                                identb[:]),
                                waits=[t_ab, aguard["pstA"]], inc="mp")
                            aguard["pstA"] = p.add(
                                "vector", lambda v, tb=tb, dc=dc: v.tensor_copy(
                                    xaT[:, dc, tb * 128:(tb + 1) * 128],
                                    pstA), waits=[tt], inc="mv")
                        t_prev = aguard["pstA"]
                    t_xaT = t_prev

                    def proj(wsrc, col0, nrows, extra=()):
                        t_wl = [None, None]
                        t_mm = None
                        for dc in range(8):
                            k = dc % 2
                            tw = p.dma("sync", wpb[k][:, 0:nrows],
                                       wsrc[dc * 128:(dc + 1) * 128,
                                            col0:col0 + nrows],
                                       waits=[t_wl[k], t_conv, t_conv2])
                            def mms(e, dc=dc, k=k, nrows=nrows):
                                last = None
                                for sj in range(4):
                                    last = e.matmul(
                                        ps[sj][0:nrows, 0:512],
                                        wpb[k][:, 0:nrows],
                                        xaT[:, dc, sj * 512:(sj + 1) * 512],
                                        start=(dc == 0), stop=(dc == 7))
                                return last
                            t_mm = p.add("tensor", mms,
                                         waits=[tw, t_xaT, aguard["psA"]] +
                                         list(extra) +
                                         ([t_mm] if t_mm else []), inc="mp")
                            t_wl[k] = t_mm
                        return t_mm

                    def rope_inplace(toks_evac):
                        t_last = None
                        for sj in range(4):
                            tsw = p.add("tensor", lambda e, sj=sj: e.matmul(
                                ps[4][:, 0:512], pswap[:],
                                praw[:, sj * 512:(sj + 1) * 512],
                                start=True, stop=True),
                                waits=[toks_evac[sj], aguard["ps4"]],
                                inc="mp")
                            t_r1 = p.add("vector", lambda v, sj=sj:
                                         v.tensor_tensor(
                                             prw2[:], ps[4][:, 0:512],
                                             ropeS[:, sj * 512:(sj + 1) * 512],
                                             op=ALU.mult),
                                         waits=[tsw], inc="mv")
                            aguard["ps4"] = t_r1
                            t_r2 = p.add("vector", lambda v, sj=sj:
                                         v.tensor_tensor(
                                             prw3[:],
                                             praw[:, sj * 512:(sj + 1) * 512],
                                             ropeC[:, sj * 512:(sj + 1) * 512],
                                             op=ALU.mult),
                                         waits=[toks_evac[sj]], inc="mv")
                            t_last = p.add("vector", lambda v, sj=sj:
                                           v.tensor_add(
                                               praw[:, sj * 512:(sj + 1) * 512],
                                               prw2[:], prw3[:]),
                                           waits=[t_r1, t_r2], inc="mv")
                        return t_last

                    for which, base, dstD in (("k", D, khD), ("q", 0, qhD)):
                        for hc in range(8):
                            t_mm = proj(wqkvT_bf, base + hc * 128, 128)
                            evs = []
                            for sj in range(4):
                                evs.append(p.add(
                                    "vector", lambda v, sj=sj: v.tensor_copy(
                                        praw[:, sj * 512:(sj + 1) * 512],
                                        ps[sj][:, 0:512]),
                                    waits=[t_mm], inc="mv"))
                            aguard["psA"] = evs[-1]
                            t_rp = rope_inplace(evs)
                            for hh in range(2):
                                h = hc * 2 + hh
                                t_sp = p.add(
                                    "vector", lambda v, hh=hh: v.tensor_copy(
                                        stg[:], praw[hh * 64:(hh + 1) * 64, :]),
                                    waits=[t_rp], inc="mv")
                                t_sd = p.dma("sync", dstD[h], stg[:],
                                             waits=[t_sp], inc="st")
                                p.add("vector", lambda v: None, waits=[t_sd])
                    vput = None
                    for hc in range(8):
                        t_mm = proj(wqkvT_bf, 2 * D + hc * 128, 128)
                        evs = []
                        for sj in range(4):
                            evs.append(p.add(
                                "vector", lambda v, sj=sj: v.tensor_copy(
                                    praw[:, sj * 512:(sj + 1) * 512],
                                    ps[sj][:, 0:512]),
                                waits=[t_mm], inc="mv"))
                        aguard["psA"] = evs[-1]
                        for tb in range(NTB):
                            tt = p.add("tensor", lambda e, tb=tb: e.transpose(
                                pstB, praw[:, tb * 128:(tb + 1) * 128],
                                identb[:]),
                                waits=[evs[tb // 4], aguard["pstB"]],
                                inc="mp")
                            for hh in range(2):
                                h = hc * 2 + hh
                                vput = p.add(
                                    "vector",
                                    lambda v, tb=tb, hh=hh, h=h: v.tensor_copy(
                                        vbuf[:, tb, h * 65:h * 65 + 64],
                                        pstB[:, hh * 64:(hh + 1) * 64]),
                                    waits=[tt], inc="mv")
                            aguard["pstB"] = vput
                    def vones(v):
                        return v.memset(
                            vbuf[:].rearrange("p a (h c) -> p a h c", c=65)
                            [:, :, :, 64:65], 1.0)
                    t_vones = p.add("vector", vones, waits=[vput], inc="mv")

                    for c_ in range(2):
                        t_mm = proj(wqiT_bf, c_ * 128, 128)
                        for hh in range(2):
                            ih = c_ * 2 + hh
                            evs = []
                            for sj in range(4):
                                evs.append(p.add(
                                    "vector",
                                    lambda v, ih=ih, hh=hh, sj=sj:
                                    v.tensor_copy(
                                        qi_sb[ih][:, sj * 512:(sj + 1) * 512],
                                        ps[sj][hh * 64:(hh + 1) * 64, 0:512]),
                                    waits=[t_mm], inc="mv"))
                            aguard["psA"] = evs[-1]
                    t_mm = proj(wkiT_bf, 0, 64)
                    evs = []
                    for sj in range(4):
                        evs.append(p.add(
                            "vector", lambda v, sj=sj: v.tensor_copy(
                                ki_sb[:, sj * 512:(sj + 1) * 512],
                                ps[sj][0:64, 0:512]), waits=[t_mm], inc="mv"))
                    aguard["psA"] = evs[-1]
                    t_mz = p.add("vector", lambda v: v.memset(wi32[:], 0.0),
                                 inc="mv")
                    t_mm = proj(wwiT_bf, 0, HI, extra=[t_mz])
                    evs = []
                    for sj in range(4):
                        evs.append(p.add(
                            "vector", lambda v, sj=sj: v.tensor_copy(
                                wi32[0:4, sj * 512:(sj + 1) * 512],
                                ps[sj][0:4, 0:512]), waits=[t_mm, t_mz],
                            inc="mv"))
                    aguard["psA"] = evs[-1]
                    for tb in range(NTB):
                        tt = p.add("tensor", lambda e, tb=tb: e.transpose(
                            psm[:, 0:32], wi32[:, tb * 128:(tb + 1) * 128],
                            identf[0:32, 0:32]),
                            waits=[evs[tb // 4], aguard["psm"]], inc="mp")
                        aguard["psm"] = p.add(
                            "vector", lambda v, tb=tb: v.tensor_copy(
                                wi_tok[:, tb, :], psm[:, 0:4]),
                            waits=[tt], inc="mv")
                p.barrier()

                mctx = ExitStack()
                with mctx:
                    sbm = lambda name, shape, dt: mctx.enter_context(
                        nc.sbuf_tensor(f"{name}_{xoff}", shape, dt))
                    allow_T = sbm("a_alT", [128, 16, 512], BF16)
                    idx1 = sbm("a_idx1", [128, S], F32)
                    awork = sbm("a_awork", [128, S], F32)
                    gwork = sbm("a_gwork", [128, S], F32)
                    tmpw = gwork
                    allow_tok = sbm("a_alt", [128, S], BF16)
                    p_m = sbm("a_pm", [128, 16 * 512], BF16)
                    khs = [sbm(f"a_khs{k}", [64, S], BF16) for k in range(2)]
                    qhs = [sbm(f"a_qhs{k}", [64, 512], BF16) for k in range(2)]
                    avT = sbm("a_avT", [128, 8, 512], BF16)
                    av64 = sbm("a_av64", [64, 512], BF16)
                    recd = sbm("a_recd", [1, 512], F32)
                    bis = sbm("a_bis", [128, 16], F32)
                    bisi = sbm("a_bisi", [128, 4], I32)
                    xr = xs2

                    t_allowT_use = None
                    t_pm_use = None
                    prev_out = None
                    for g in range(4):
                        t0g = g * 512
                        if g == 0:
                            def seed(v):
                                v.memset(allow_T[:, 0:4, :], 1.0)
                                return v.memset(allow_T[:, 4:16, :], 0.0)
                            t_alT = p.add("vector", seed,
                                          waits=[t_allowT_use], inc="mv")
                        else:
                            tr_done = []
                            for q in range(4):
                                ttile = 4 * g + q
                                t0 = ttile * 128
                                W = t0 + 128
                                tprev_ih = None
                                tprev_add = None
                                for ih in range(4):
                                    def dmm(e, ih=ih, ttile=ttile):
                                        last = None
                                        for sj in range(4):
                                            last = e.matmul(
                                                ps[sj][:, 0:512],
                                                qi_sb[ih][:, ttile * 128:
                                                          ttile * 128 + 128],
                                                ki_sb[:, sj * 512:
                                                      (sj + 1) * 512],
                                                start=True, stop=True)
                                        return last
                                    t_dm = p.add("tensor", dmm,
                                                 waits=[aguard["psA"],
                                                        tprev_ih], inc="mp")
                                    if ih == 0:
                                        def as0(v, ttile=ttile, ih=ih):
                                            last = None
                                            for sj in range(4):
                                                last = v.tensor_scalar(
                                                    idx1[:, sj * 512:
                                                         (sj + 1) * 512],
                                                    ps[sj][:, 0:512], 0.0,
                                                    wi_tok[:, ttile, ih:ih + 1],
                                                    op0=ALU.max, op1=ALU.mult)
                                            return last
                                        t_as = p.add("vector", as0,
                                                     waits=[t_dm], inc="mv")
                                    else:
                                        def asx(v, ttile=ttile, ih=ih):
                                            last = None
                                            for sj in range(4):
                                                last = v.tensor_scalar(
                                                    tmpw[:, sj * 512:
                                                         (sj + 1) * 512],
                                                    ps[sj][:, 0:512], 0.0,
                                                    wi_tok[:, ttile, ih:ih + 1],
                                                    op0=ALU.max, op1=ALU.mult)
                                            return last
                                        t_ts = p.add("vector", asx,
                                                     waits=[t_dm, tprev_add],
                                                     inc="mv")
                                        t_as = p.add("gpsimd",
                                                     lambda gp: gp.tensor_add(
                                                         idx1[:], idx1[:],
                                                         tmpw[:]),
                                                     waits=[t_ts, tprev_add],
                                                     inc="mg")
                                    tprev_add = t_as
                                    aguard["psA"] = t_as
                                    tprev_ih = t_as
                                t_c1 = p.add("vector", lambda v, t0=t0:
                                             v.tensor_mul(
                                                 idx1[:, t0:t0 + 128],
                                                 idx1[:, t0:t0 + 128],
                                                 tri01[:]),
                                             waits=[tprev_ih], inc="mv")
                                t_c2 = p.add("vector", lambda v, t0=t0:
                                             v.tensor_add(
                                                 idx1[:, t0:t0 + 128],
                                                 idx1[:, t0:t0 + 128],
                                                 trineg[:]),
                                             waits=[t_c1], inc="mv")
                                lo1, hi1 = bis[:, 0:1], bis[:, 1:2]
                                mid1, cnt1 = bis[:, 2:3], bis[:, 3:4]
                                need1 = bis[:, 4:5]
                                def binit(v):
                                    v.memset(lo1, -BIS_B)
                                    v.memset(hi1, BIS_B)
                                    return v.memset(mid1, 0.0)
                                t_bi = p.add("vector", binit, waits=[t_c2],
                                             inc="mv")
                                for it in range(BIS_J):
                                    t_cn = p.add(
                                        "vector", lambda v, W=W:
                                        v.tensor_scalar(
                                            gwork[:, 0:W], idx1[:, 0:W],
                                            mid1, 0.0, op0=ALU.is_gt,
                                            op1=ALU.add, accum_out=cnt1),
                                        waits=[t_bi], inc="mv")
                                    t_se = p.add(
                                        "vector", lambda v: v.tensor_scalar(
                                            bisi[:, 0:1], cnt1, float(KSEL),
                                            None, op0=ALU.is_ge),
                                        waits=[t_cn], inc="mv")
                                    def selstep(v):
                                        v.select(lo1, bisi[:, 0:1], mid1, lo1)
                                        return v.select(hi1, bisi[:, 0:1],
                                                        hi1, mid1)
                                    t_sl_ = p.add("vector", selstep,
                                                  waits=[t_se], inc="mv")
                                    t_ad = p.add("vector",
                                                 lambda v: v.tensor_add(
                                                     mid1, lo1, hi1),
                                                 waits=[t_sl_], inc="mv")
                                    t_bi = p.add("vector",
                                                 lambda v: v.tensor_scalar_mul(
                                                     mid1, mid1, 0.5),
                                                 waits=[t_ad], inc="mv")
                                t_a1 = p.add("vector", lambda v, W=W:
                                             v.tensor_scalar(
                                                 awork[:, 0:W], idx1[:, 0:W],
                                                 hi1, 0.0, op0=ALU.is_gt,
                                                 op1=ALU.add,
                                                 accum_out=cnt1),
                                             waits=[t_bi], inc="mv")
                                t_nd = p.add("vector", lambda v:
                                             v.tensor_scalar(
                                                 need1, cnt1, -1.0,
                                                 float(KSEL), op0=ALU.mult,
                                                 op1=ALU.add),
                                             waits=[t_a1], inc="mv")
                                t_gl = p.add("vector", lambda v, W=W:
                                             v.tensor_scalar(
                                                 gwork[:, 0:W], idx1[:, 0:W],
                                                 lo1, None, op0=ALU.is_gt),
                                             waits=[t_a1], inc="mv")
                                t_cd = p.add("vector", lambda v, W=W:
                                             v.tensor_sub(
                                                 gwork[:, 0:W], gwork[:, 0:W],
                                                 awork[:, 0:W]),
                                             waits=[t_gl], inc="mv")
                                t_pc = p.add("vector", lambda v, W=W:
                                             v.tensor_tensor_scan(
                                                 idx1[:, 0:W], gwork[:, 0:W],
                                                 gwork[:, 0:W], 0.0,
                                                 op0=ALU.add, op1=ALU.bypass),
                                             waits=[t_cd], inc="mv")
                                t_ok = p.add("vector", lambda v, W=W:
                                             v.scalar_tensor_tensor(
                                                 gwork[:, 0:W], idx1[:, 0:W],
                                                 need1, gwork[:, 0:W],
                                                 op0=ALU.is_le, op1=ALU.mult),
                                             waits=[t_pc, t_nd], inc="mv")
                                t_al = p.add("vector", lambda v, W=W:
                                             v.tensor_add(
                                                 allow_tok[:, 0:W],
                                                 gwork[:, 0:W],
                                                 awork[:, 0:W]),
                                             waits=[t_ok], inc="mv")
                                if W < S:
                                    t_al = p.add("vector", lambda v, W=W:
                                                 v.memset(
                                                     allow_tok[:, W:S], 0.0),
                                                 waits=[t_al], inc="mv")
                                for sc in range(16):
                                    tt = p.add("tensor",
                                               lambda e, sc=sc: e.transpose(
                                                   pstA,
                                                   allow_tok[:, sc * 128:
                                                             (sc + 1) * 128],
                                                   identb[:]),
                                               waits=[t_al, aguard["pstA"],
                                                      t_allowT_use],
                                               inc="mp")
                                    aguard["pstA"] = p.add(
                                        "vector",
                                        lambda v, sc=sc, q=q: v.tensor_copy(
                                            allow_T[:, sc,
                                                    q * 128:(q + 1) * 128],
                                            pstA), waits=[tt], inc="mv")
                                    tr_done.append(aguard["pstA"])
                            t_alT = tr_done[-1]

                        t_scores_prev = None
                        for h in range(H):
                            par = h % 2
                            t_kl = p.dma("sync", khs[par][:], khD[h],
                                         waits=[prev_kq_use[par]])
                            t_ql = p.dma("sync", qhs[par][:],
                                         qhD[h][:, t0g:t0g + 512],
                                         waits=[prev_kq_use[par]])
                            fill_ev = [None] * 4
                            for f in range(4):
                                def smm(e, f=f, par=par):
                                    last = None
                                    for c_ in range(4):
                                        sc = 4 * f + c_
                                        last = e.matmul(
                                            ps[c_][:, 0:512],
                                            khs[par][:, sc * 128:
                                                     (sc + 1) * 128],
                                            qhs[par][:], start=True,
                                            stop=True)
                                    return last
                                waits = [t_kl, t_ql, aguard["psA"], t_alT,
                                         t_pm_use]
                                t_sm = p.add("tensor", smm, waits=waits,
                                             inc="mp")
                                t_ex = p.add("scalar", lambda s, f=f:
                                             s.activation(
                                                 p_m[:, f * 2048:
                                                     (f + 1) * 2048],
                                                 psA[:, 0:2048], AF.Exp,
                                                 scale=cvals[:, 4:5]),
                                             waits=[t_sm, t_cv], inc="ms")
                                aguard["psA"] = t_ex
                                t_mk = p.add("vector", lambda v, f=f:
                                             v.tensor_tensor(
                                                 p_m[:, f * 2048:
                                                     (f + 1) * 2048],
                                                 p_m[:, f * 2048:
                                                     (f + 1) * 2048],
                                                 allow_T[:].rearrange(
                                                     "p a b -> p (a b)")
                                                 [:, f * 2048:(f + 1) * 2048],
                                                 op=ALU.mult),
                                             waits=[t_ex], inc="mv")
                                fill_ev[f] = t_mk
                            def avmm2(e, h=h):
                                last = None
                                for sc in range(16):
                                    last = e.matmul(
                                        ps[4][0:65, 0:512],
                                        vbuf[:, sc, h * 65:h * 65 + 65],
                                        p_m[:, sc * 512:(sc + 1) * 512],
                                        start=(sc == 0), stop=(sc == 15))
                                return last
                            t_av = p.add("tensor", avmm2,
                                         waits=fill_ev + [t_vones,
                                                          aguard["ps4"]],
                                         inc="mp")
                            t_pm_use = t_av
                            prev_kq_use[par] = t_av
                            t_rc = p.add("vector", lambda v: v.reciprocal(
                                recd[:], ps[4][64:65, 0:512]),
                                waits=[t_av], inc="mv")
                            t_bc2 = p.add("tensor", lambda e: e.matmul(
                                ps[5][0:64, 0:512], ones_row[:, 0:64],
                                recd[:], start=True, stop=True),
                                waits=[t_rc, aguard["ps5"]], inc="mp")
                            t_cp = p.add("vector", lambda v: v.tensor_copy(
                                av64[:], ps[4][0:64, 0:512]),
                                waits=[t_av], inc="mv")
                            aguard["ps4"] = t_cp
                            hc, hh = h // 2, h % 2
                            t_nm2 = p.add("vector",
                                          lambda v, hc=hc, hh=hh:
                                          v.tensor_tensor(
                                              avT[hh * 64:(hh + 1) * 64,
                                                  hc, :],
                                              av64[:], ps[5][0:64, 0:512],
                                              op=ALU.mult),
                                          waits=[t_bc2, t_cp,
                                                 prev_out], inc="mv")
                            aguard["ps5"] = t_nm2
                            av_done[h] = t_nm2

                        outs = []
                        for tb in range(4):
                            def omm(e, tb=tb):
                                last = None
                                for dh in range(2):
                                    for hc in range(8):
                                        last = e.matmul(
                                            ps[dh][:, 0:512],
                                            avT[:, hc, tb * 128:
                                                (tb + 1) * 128],
                                            woT_sb[:, hc * D + dh * 512:
                                                   hc * D + dh * 512 + 512],
                                            start=(hc == 0), stop=(hc == 7))
                                return last
                            t_om = p.add("tensor", omm,
                                         waits=list(av_done.values()) +
                                         [aguard["psA"], t_wo], inc="mp")
                            r0 = xoff + t0g + tb * 128
                            t_xl3 = p.dma("sync", xr[:], xbuf[r0:r0 + 128, :],
                                          waits=[outs[-1] if outs else None])
                            def radd(v):
                                v.scalar_tensor_tensor(
                                    xr[:, 0:512], ps[0][:, 0:512], 1.0,
                                    xr[:, 0:512], op0=ALU.bypass, op1=ALU.add)
                                return v.scalar_tensor_tensor(
                                    xr[:, 512:1024], ps[1][:, 0:512], 1.0,
                                    xr[:, 512:1024], op0=ALU.bypass,
                                    op1=ALU.add)
                            t_ra = p.add("vector", radd, waits=[t_om, t_xl3],
                                         inc="mv")
                            aguard["psA"] = t_ra
                            t_st3 = p.dma("sync", xbuf[r0:r0 + 128, :], xr[:],
                                          waits=[t_ra], inc="st")
                            outs.append(t_st3)
                        prev_out = outs[-1]
                        t_allowT_use = t_pm_use

        prev_kq_use = [None, None]
        av_done = {}

        if stage == "s0":
            p.dma("sync", x_out[0:128, 0:8], cvals[:],
                  waits=[t_conv, t_conv2], inc="st")
        elif stage in ("f1", "s1", "s1x", "s1a", "s1b"):
            with ExitStack() as stk:
                ffn_call(0, x_in, x_out, stk)
        elif stage == "f1a":
            with ExitStack() as stk:
                ffn_call(0, x_in, xbuf, stk)
            p.barrier()
            with ExitStack() as stk:
                attention(stk)
            p.barrier()
            p.dma("sync", x_out, xbuf, inc="st")
        else:
            with ExitStack() as stk:
                ffn_call(0, x_in, xbuf, stk)
            p.barrier()
            with ExitStack() as stk:
                attention(stk)
            p.barrier()
            with ExitStack() as stk:
                ffn_call(1, xbuf, x_out, stk)

        p.add("sync", lambda e: None,
              waits=[("st", p.counts.get("st", 0)),
                     ("ld", p.counts.get("ld", 0))])
        p.add("gpsimd", lambda e: None,
              waits=[("gp", p.counts.get("gp", 0))])

        block = Es(nc.Block())
        p.emit(block)
    return nc


def _prep_inputs(inputs):
    T = lambda a: np.ascontiguousarray(a.T)
    full = {
        "wqkvT": T(inputs["L0_attn_w_qkv"]),
        "woT": T(inputs["L0_attn_w_o"]),
        "wqiT": T(inputs["L0_attn_w_qi"]),
        "wkiT": T(inputs["L0_attn_w_ki"]),
        "wwiT": T(inputs["L0_attn_w_wi"]),
        "wr_in": np.ascontiguousarray(inputs["L0_ffn_Wr"]),
        "br_in": inputs["L0_ffn_br"].reshape(1, E),
        "lng_in": inputs["L0_ffn_ln_g"].reshape(1, D),
        "lnb_in": inputs["L0_ffn_ln_b"].reshape(1, D),
        "w1_in": inputs["L0_ffn_W1"],
        "b1_in": inputs["L0_ffn_b1"],
        "w2_in": inputs["L0_ffn_W2"],
        "b2_in": inputs["L0_ffn_b2"],
    }
    x = inputs["L0_attn_x"]
    in_maps = []
    for c in range(NCORES):
        m = dict(full)
        m["x_in"] = np.ascontiguousarray(
            x[c * BPC:(c + 1) * BPC].reshape(TOK, D))
        in_maps.append(m)
    return in_maps


def kernel(**inputs):
    nc = build("full")
    in_maps = _prep_inputs(inputs)
    res = run_bass_kernel_spmd(nc, in_maps, core_ids=list(range(NCORES)))
    out = np.empty((B, S, D), np.float32)
    for c in range(NCORES):
        out[c * BPC:(c + 1) * BPC] = res.results[c]["x_out"].reshape(BPC, S, D)
    return out
